# version 07
# baseline (speedup 1.0000x reference)
import numpy as np
import concourse.bass as bass
import concourse.mybir as mybir
from concourse.bass_utils import run_bass_kernel_spmd

F32 = mybir.dt.float32
F16 = mybir.dt.float16
F8 = mybir.dt.float8e4
AF = mybir.ActivationFunctionType
ALU = mybir.AluOpType
AX = mybir.AxisListType.X

P = 128
T = 512
NCH = 4096 // T
DM = 1024
DI = 2048
DS = 16
HF = 4096
DB = DI // P
MB = DM // P
HB = HF // P


class Prog:
    def __init__(self, nc):
        self.nc = nc
        self.cmds = []
        self.sems = {}

    def _sem(self, name):
        if name not in self.sems:
            self.sems[name] = [self.nc.alloc_semaphore(name), 0]
        return self.sems[name]

    def emit(self, eng, fn, inc=None, by=1):
        if inc is None:
            self.cmds.append((eng, fn))
            return None
        rec = self._sem(inc)
        rec[1] += by
        tgt, h = rec[1], rec[0]

        def wrapped(e, fn=fn, h=h, by=by):
            fn(e).then_inc(h, by)
        self.cmds.append((eng, wrapped))
        return (inc, tgt)

    def dma(self, out, in_, inc='qw', eng='q', slow=False):
        rec = self._sem(inc)
        rec[1] += 16
        tgt, h = rec[1], rec[0]

        def fn(e, out=out, in_=in_, h=h, slow=slow):
            e.dma_start(out=out, in_=in_, allow_slow_non_contiguous=slow).then_inc(h, 16)
        self.cmds.append((eng, fn))
        return (inc, tgt)

    def wait(self, eng, name, val=None):
        if isinstance(name, tuple):
            name, val = name
        if name is None:
            return
        rec = self._sem(name)
        v = rec[1] if val is None else val
        if v > 0:
            self.cmds.append((eng, lambda e, h=rec[0], v=v: e.wait_ge(h, v)))

    def cur(self, name):
        return (name, self._sem(name)[1])


def build_nc():
    nc = bass.Bass("TRN2", target_bir_lowering=False, debug=False)

    xin = nc.dram_tensor("x", [4096, DM], F32, kind="ExternalInput").ap()
    w_in = nc.dram_tensor("w_in", [DM, 2 * DI], F32, kind="ExternalInput").ap()
    b_in = nc.dram_tensor("b_in", [2 * DI], F32, kind="ExternalInput").ap()
    w_conv = nc.dram_tensor("w_conv", [DI, 4], F32, kind="ExternalInput").ap()
    b_conv = nc.dram_tensor("b_conv", [DI], F32, kind="ExternalInput").ap()
    w_x = nc.dram_tensor("w_x", [DI, DI + 2 * DS], F32, kind="ExternalInput").ap()
    b_x = nc.dram_tensor("b_x", [DI + 2 * DS], F32, kind="ExternalInput").ap()
    w_dt = nc.dram_tensor("w_dt", [DI, DI], F32, kind="ExternalInput").ap()
    b_dt = nc.dram_tensor("b_dt", [DI], F32, kind="ExternalInput").ap()
    a_log = nc.dram_tensor("a_log", [DI, DS], F32, kind="ExternalInput").ap()
    d_par = nc.dram_tensor("d_par", [DI], F32, kind="ExternalInput").ap()
    w_out = nc.dram_tensor("w_out", [DI, DM], F32, kind="ExternalInput").ap()
    b_out = nc.dram_tensor("b_out", [DM], F32, kind="ExternalInput").ap()
    gam = nc.dram_tensor("gam", [DM], F32, kind="ExternalInput").ap()
    bet = nc.dram_tensor("bet", [DM], F32, kind="ExternalInput").ap()
    w1 = nc.dram_tensor("w1", [HF, DM], F32, kind="ExternalInput").ap()
    b1 = nc.dram_tensor("b1", [HF], F32, kind="ExternalInput").ap()
    w2 = nc.dram_tensor("w2", [DM, HF], F32, kind="ExternalInput").ap()
    b2 = nc.dram_tensor("b2", [DM], F32, kind="ExternalInput").ap()
    aux = nc.dram_tensor("aux", [P, P + 1], F32, kind="ExternalInput").ap()
    yout = nc.dram_tensor("out", [4096, DM], F32, kind="ExternalOutput").ap()

    w_in16s = nc.dram_tensor("w_in16s", [16, P, 2 * MB * P], F16, kind="Internal").ap()
    w_cmb16s = nc.dram_tensor("w_cmb16s", [DB, P, DB * P], F16, kind="Internal").ap()
    w_outs = nc.dram_tensor("w_outs", [MB, P, DB * P], F16, kind="Internal").ap()
    w1s = nc.dram_tensor("w1s", [DB, P, 2 * MB * P], F16, kind="Internal").ap()
    w2s = nc.dram_tensor("w2s", [MB, P, HB * P], F8, kind="Internal").ap()
    wdt16 = nc.dram_tensor("wdt16", [DI, DI], F16, kind="Internal").ap()
    wxd16 = nc.dram_tensor("wxd16", [DI, DI], F16, kind="Internal").ap()

    A = nc.alloc_sbuf_tensor
    id16 = A("id16", [P, P], F16)
    idf32 = A("idf32", [P, P], F32)
    ones_r = A("ones_r", [1, P], F16)
    ones_t = A("ones_t", [1, T], F16)
    onek16 = A("onek16", [P, 1], F16)
    nhalf = A("nhalf", [P, 1], F32)
    ind32 = A("ind32", [32, 32 * P], F16)
    acol = A("acol", [P, DB * DS], F32)
    dcol = A("dcol", [P, DB], F32)
    thr = A("thr", [P, DB], F32)
    binc = A("binc", [P, 2 * DB], F32)
    bcmb = A("bcmb", [P, DB], F32)
    bdtc = A("bdtc", [P, DB], F32)
    bxbr = A("bxbr", [1, 32], F16)
    bxdc = A("bxdc", [P, DB], F16)
    boutc = A("boutc", [P, MB], F32)
    gamc = A("gamc", [P, MB], F32)
    betc = A("betc", [P, MB], F32)
    b1c = A("b1c", [P, HB], F32)
    b2q16 = A("b2q16", [P, MB], F16)
    b2qc = A("b2qc", [P, MB], F32)
    wcv = A("wcv", [P, 4 * DB], F32)
    hstate = A("hstate", [P, DB * DS], F32)
    stats = A("stats", [P, 32], F32)
    tails = A("tails", [P, DB * 3], F16)
    diag = A("diag", [P, 8 * P], F16)

    xbuf = A("xbuf", [P, 2 * DM], F32)
    xn16 = A("xn16", [P, 4 * DM], F16)
    xnt = A("xnt", [P, MB * T], F16)
    stage = A("stage", [P, DB * (T + 3)], F16)
    wbig = A("wbig", [P, 4 * DB * T], F16)
    srA = wbig[:, 0 * DB * T:1 * DB * T]
    spkA = wbig[:, 1 * DB * T:2 * DB * T]
    d2A = wbig[:, 2 * DB * T:3 * DB * T]
    wbA = wbig[:, 3 * DB * T:4 * DB * T]
    bcb = A("bcb", [32, T], F16)
    brep = A("brep", [P, 2 * T], F16)
    crep = A("crep", [P, 2 * T], F16)
    abar = A("abar", [P, 4 * T], F16)
    bxs = A("bxs", [P, 4 * T], F16)
    hsc = A("hsc", [P, 2 * T], F16)
    hcs = A("hcs", [P, 2 * T], F16)
    yac = A("yac", [P, DB * T], F16)
    ob = A("ob", [P, MB * T], F16)
    o2t = A("o2t", [P, 2 * T], F16)
    sb16 = A("sb16", [P, MB * T], F16)
    st1 = A("st1", [1, 2 * T], F32)
    st2 = A("st2", [1, 2 * T], F16)
    tmpa = A("tmpa", [P, 2 * T], F16)
    winr = A("winr", [P, 2 * 2 * MB * P], F16)
    wcmr = A("wcmr", [P, 2 * DB * P], F16)
    wor = A("wor", [P, 2 * DB * P], F16)
    wxbcr = A("wxbcr", [P, DB * 32], F16)

    w1r = winr
    oqb = winr
    sgt = yac
    rhs_strip = yac
    hqb = stage[:, 0:HB * T // 2].bitcast(F8)
    w2r = wcmr[:].bitcast(F8)
    hq8s = sb16[:].bitcast(F8)

    pm0 = nc.alloc_psum_tensor("pm0", [P, T], F32)
    pm1 = nc.alloc_psum_tensor("pm1", [P, T], F32)
    pr0 = nc.alloc_psum_tensor("pr0", [P, T], F32)
    pr1 = nc.alloc_psum_tensor("pr1", [P, T], F32)
    pb0 = nc.alloc_psum_tensor("pb0", [P, T], F32)
    pb1 = nc.alloc_psum_tensor("pb1", [P, T], F32)
    pt0 = nc.alloc_psum_tensor("pt0", [P, 1024], F16)
    pt1 = nc.alloc_psum_tensor("pt1", [P, 1024], F16)

    pg = Prog(nc)
    E, W, D = pg.emit, pg.wait, pg.dma

    NOG = (None, 0)
    G = {'pm': [NOG, NOG], 'pr': [NOG, NOG], 'pb': [NOG, NOG], 'pt': [NOG, NOG]}

    def mm_chain(pmi, lhs_fn, rhs_fn, n, nfree=T, rows=None, extra_wait=None, pair='pm'):
        pm = (pm0, pm1)[pmi] if pair == 'pm' else (pb0, pb1)[pmi]
        W('t', G[pair][pmi])
        if extra_wait is not None:
            W('t', extra_wait)
        tt = None
        for k in range(n):
            tt = E('t', lambda t, k=k, pm=pm: t.matmul(
                out=pm[:, 0:nfree] if rows is None else pm[0:rows, 0:nfree],
                lhsT=lhs_fn(k), rhs=rhs_fn(k),
                start=(k == 0), stop=(k == n - 1)), inc='tm' if k == n - 1 else None)
        return tt, pm

    tgt = D(xbuf[:, 0:P + 1], aux, 'qx')
    W('v', tgt)
    E('v', lambda v: v.tensor_copy(out=id16[:], in_=xbuf[:, 0:P]))
    E('v', lambda v: v.tensor_copy(out=idf32[:], in_=xbuf[:, 0:P]))
    E('v', lambda v: v.tensor_copy(out=onek16[:], in_=xbuf[:, P:P + 1]))
    E('v', lambda v: v.tensor_copy(out=ones_r[:], in_=xbuf[0:1, P:P + 1].broadcast_to([1, P])))
    E('v', lambda v: v.tensor_copy(out=ones_t[:], in_=xbuf[0:1, P:P + 1].broadcast_to([1, T])))
    E('v', lambda v: v.memset(nhalf[:], -0.5))
    E('v', lambda v: v.memset(hstate[:], 0.0))
    E('v', lambda v: v.memset(stage[:], 0.0))
    vt = E('v', lambda v: v.memset(ind32[:], 1.0), inc='vp')
    W('g', vt)
    for s_i in range(32):
        E('g', lambda g_, s_i=s_i: g_.affine_select(
            out=ind32[:, s_i * P:(s_i + 1) * P], in_=ind32[:, s_i * P:(s_i + 1) * P],
            pattern=[[0, P]], compare_op=ALU.is_equal, fill=0.0,
            base=-s_i, channel_multiplier=1), inc='gp' if s_i == 31 else None)

    W('t', pg.cur('vp'))

    def cv(src2d, nb, sink):
        W('q', pg.cur('tp'))
        tgt = D(xbuf[0:nb, DM:DM + P], src2d, 'qx')
        W('t', tgt)
        W('t', G['pb'][0])
        tt = E('t', lambda t: t.transpose(
            out=pb0[:, 0:P], in_=xbuf[:, DM:DM + P], identity=idf32[:]), inc='tp')
        W('v', tt)
        G['pb'][0] = E('v', sink, inc='vp')

    cv(b_conv.rearrange("(a b) -> a b", b=P), DB,
       lambda v: v.tensor_scalar(out=thr[:], in0=pb0[:, 0:DB], scalar1=-1.0, scalar2=1.0,
                                 op0=ALU.mult, op1=ALU.add))
    cv(d_par.rearrange("(a b) -> a b", b=P), DB,
       lambda v: v.tensor_copy(out=dcol[:], in_=pb0[:, 0:DB]))
    cv(b_in.rearrange("(a b) -> a b", b=P), 2 * DB,
       lambda v: v.tensor_copy(out=binc[:], in_=pb0[:, 0:2 * DB]))
    cv(b_out.rearrange("(a b) -> a b", b=P), MB,
       lambda v: v.tensor_copy(out=boutc[:], in_=pb0[:, 0:MB]))
    cv(gam.rearrange("(a b) -> a b", b=P), MB,
       lambda v: v.tensor_copy(out=gamc[:], in_=pb0[:, 0:MB]))
    cv(bet.rearrange("(a b) -> a b", b=P), MB,
       lambda v: v.tensor_copy(out=betc[:], in_=pb0[:, 0:MB]))
    cv(b1.rearrange("(a b) -> a b", b=P), HB,
       lambda v: v.tensor_copy(out=b1c[:], in_=pb0[:, 0:HB]))
    cv(b2.rearrange("(a b) -> a b", b=P), MB,
       lambda v: v.tensor_copy(out=b2q16[:], in_=pb0[:, 0:MB]))
    W('s', pg.cur('vp'))
    E('s', lambda s: s.activation(out=b2qc[:], in_=b2q16[:], func=AF.Copy), inc='sp')
    cv(b_x[0:DI].rearrange("(a b) -> a b", b=P), DB,
       lambda v: v.tensor_copy(out=bxdc[:], in_=pb0[:, 0:DB]))
    cv(b_dt.rearrange("(a b) -> a b", b=P), DB,
       lambda v: v.tensor_copy(out=bdtc[:], in_=pb0[:, 0:DB]))
    tgt = D(wcv[:].rearrange("p (db t) -> p db t", t=4),
            w_conv.rearrange("(db p) t -> p db t", p=P), 'qx')
    tgt = D(xbuf[0:1, 0:32], b_x[DI:DI + 2 * DS].rearrange("(a b) -> a b", a=1), 'qx')
    W('v', tgt)
    E('v', lambda v: v.tensor_copy(out=bxbr[:], in_=xbuf[0:1, 0:32]), inc='vp')
    for db in range(DB):
        tgt = D(xbuf[:, DM + 256 + db * DS:DM + 256 + (db + 1) * DS],
                a_log[db * P:(db + 1) * P, :], 'qx')
    W('s', pg.cur('qx'))
    st_ = E('s', lambda s: s.activation(out=xbuf[:, DM + 512:DM + 512 + DB * DS],
                                        in_=xbuf[:, DM + 256:DM + 256 + DB * DS], func=AF.Exp), inc='sp')
    W('v', st_)
    E('v', lambda v: v.tensor_scalar(out=acol[:], in0=xbuf[:, DM + 512:DM + 512 + DB * DS],
                                     scalar1=-1.0, scalar2=None, op0=ALU.mult))
    E('v', lambda v: v.engine_nop(), inc='vp')

    def cast_rows(src, dst, rows, cols):
        for rb in range(rows // P):
            for cc in range(0, cols, 2048):
                w = min(2048, cols - cc)
                tgt = D(xbuf[:, 0:w], src[rb * P:(rb + 1) * P, cc:cc + w], 'qw')
                W('v', tgt)
                vt = E('v', lambda v, w=w: v.tensor_copy(out=xn16[:, 0:w], in_=xbuf[:, 0:w]), inc='vp')
                W('q', vt)
                tgt2 = D(dst[rb * P:(rb + 1) * P, cc:cc + w], xn16[:, 0:w], 'qs')
                W('v', tgt2)

    W('q', pg.cur('sp'))
    W('q', pg.cur('vp'))
    for rb in range(DM // P):
        for cc in range(0, 2 * DI, 2048):
            W('q', pg.cur('vp'))
            tgt = D(xbuf[:, 0:2048], w_in[rb * P:(rb + 1) * P, cc:cc + 2048], 'qw')
            W('v', tgt)
            vt = E('v', lambda v: v.tensor_copy(out=xn16[:, 0:2048], in_=xbuf[:, 0:2048]), inc='vp')
            W('q', vt)
            for j in range(8):
                mp = cc // 256 + j
                tgt2 = D(w_in16s[mp, :, rb * 256:(rb + 1) * 256], xn16[:, j * 256:(j + 1) * 256], 'qs')
            W('v', tgt2)
    cast_rows(w_dt, wdt16, DI, DI)
    for rb in range(DI // P):
        W('q', pg.cur('vp'))
        tgt = D(xbuf[:, 0:DM], w_out[rb * P:(rb + 1) * P, :], 'qw')
        W('v', tgt)
        vt = E('v', lambda v: v.tensor_copy(out=xn16[:, 0:DM], in_=xbuf[:, 0:DM]), inc='vp')
        W('q', vt)
        for mb in range(MB):
            tgt2 = D(w_outs[mb, :, rb * P:(rb + 1) * P], xn16[:, mb * P:(mb + 1) * P], 'qs')
        W('v', tgt2)
    for rb in range(DB):
        W('q', pg.cur('vp'))
        tgt = D(xbuf[:, 0:2048], w_x[rb * P:(rb + 1) * P, 0:2048], 'qw')
        W('v', tgt)
        vt = E('v', lambda v: v.tensor_copy(out=xn16[:, 0:DI], in_=xbuf[:, 0:DI]), inc='vp')
        W('q', vt)
        tgt2 = D(wxd16[rb * P:(rb + 1) * P, :], xn16[:, 0:DI], 'qs')
        tgt3 = D(xbuf[:, 0:32], w_x[rb * P:(rb + 1) * P, DI:DI + 32], 'qw')
        W('v', tgt3)
        W('v', tgt2)
        E('v', lambda v, rb=rb: v.tensor_copy(out=wxbcr[:, rb * 32:(rb + 1) * 32],
                                              in_=xbuf[:, 0:32]), inc='vp')

    def transpose_block(src_f16, n_tiles, sink, base):
        j = 0
        last = None
        while j < n_tiles:
            batch = min(8, n_tiles - j)
            pti = ((base + j) // 8) % 2
            pt = (pt0, pt1)[pti]
            W('t', G['pt'][pti])
            for jj in range(batch):
                tt = E('t', lambda t, jj=jj, j=j, pt=pt, src=src_f16: t.transpose(
                    out=pt[:, jj * P:(jj + 1) * P], in_=src[:, (j + jj) * P:(j + jj + 1) * P],
                    identity=id16[:]), inc='tp' if jj == batch - 1 else None)
            W('v', tt)
            for jj in range(batch):
                vt = E('v', lambda v, jj=jj, j=j, pt=pt: sink(v, j + jj, pt[:, jj * P:(jj + 1) * P]),
                       inc='vp' if jj == batch - 1 else None)
            G['pt'][pti] = vt
            last = vt
            j += batch
        return last

    for hb in range(HB):
        W('q', pg.cur('vp'))
        tgt = D(xbuf[:, 0:DM], w1[hb * P:(hb + 1) * P, :], 'qw')
        W('v', tgt)
        E('v', lambda v: v.tensor_copy(out=xn16[:, 0:DM], in_=xbuf[:, 0:DM]), inc='vp')
        W('t', pg.cur('vp'))
        transpose_block(xn16[:, 0:DM], MB,
                        (lambda hb: lambda v, kb, ps: v.tensor_copy(
                            out=wbig[:, kb * HF + hb * P:kb * HF + (hb + 1) * P], in_=ps))(hb),
                        hb * MB)
    W('q', pg.cur('vp'))
    for kb in range(MB):
        for mp in range(DB):
            D(w1s[mp, :, kb * 256:(kb + 1) * 256],
              wbig[:, kb * HF + mp * 256:kb * HF + (mp + 1) * 256], 'qs')
    W('v', pg.cur('qs'))

    for rb in range(MB):
        for cc in range(0, HF, 2048):
            W('q', pg.cur('vp'))
            tgt = D(xbuf[:, 0:2048], w2[rb * P:(rb + 1) * P, cc:cc + 2048], 'qw')
            W('v', tgt)
            E('v', lambda v, cc=cc: v.tensor_copy(out=hq8s[:, cc:cc + 2048], in_=xbuf[:, 0:2048]), inc='vp')
        E('v', lambda v: v.tensor_copy(out=xn16[:, 0:HF], in_=hq8s[:, 0:HF]), inc='vp')
        W('t', pg.cur('vp'))
        transpose_block(xn16[:, 0:HF], HB,
                        (lambda rb: lambda v, hb, ps: v.tensor_copy(
                            out=wbig[:, (hb * MB + rb) * P:(hb * MB + rb + 1) * P], in_=ps))(rb),
                        rb * HB)
    for hb in range(HB):
        vt = E('v', lambda v, hb=hb: v.tensor_copy(
            out=hq8s[:, 0:DM], in_=wbig[:, hb * MB * P:(hb + 1) * MB * P]), inc='vp')
        W('q', vt)
        for mb in range(MB):
            tgt2 = D(w2s[mb, :, hb * P:(hb + 1) * P], hq8s[:, mb * P:(mb + 1) * P], 'qs')
        W('v', tgt2)

    for ib in range(DB):
        W('q', pg.cur('tp'))
        tgt = D(xn16[:, 0:DI], wxd16[ib * P:(ib + 1) * P, :], 'qw')
        W('t', tgt)
        transpose_block(xn16[:, 0:DI], DB,
                        (lambda ib: lambda v, kb, ps: v.tensor_copy(
                            out=wbig[:, (kb * DB + ib) * P:(kb * DB + ib + 1) * P], in_=ps))(ib),
                        ib * DB)

    for jt in range(DI // T):
        W('q', pg.cur('tm'))
        for kb in range(DB):
            tgt = D(rhs_strip[:, kb * T:(kb + 1) * T],
                    wdt16[kb * P:(kb + 1) * P, jt * T:(jt + 1) * T], 'qw')
        W('t', tgt)
        for jb_in in range(4):
            jb = jt * 4 + jb_in
            pmi = jb % 2
            tt, pm = mm_chain(pmi,
                              lambda k, jb_in=jb_in: rhs_strip[:, k * T + jb_in * P:k * T + jb_in * P + P],
                              lambda k: bxdc[:, k:k + 1], DB, nfree=1)
            W('v', tt)
            G['pm'][pmi] = E('v', lambda v, jb=jb, pm=pm: v.tensor_tensor(
                out=bcmb[:, jb:jb + 1], in0=pm[:, 0:1], in1=bdtc[:, jb:jb + 1], op=ALU.add), inc='vp')
        for ib in range(DB):
            pmi = ib % 2
            tt, pm = mm_chain(pmi,
                              lambda k, ib=ib: wbig[:, (k * DB + ib) * P:(k * DB + ib + 1) * P],
                              lambda k: rhs_strip[:, k * T:(k + 1) * T], DB)
            W('v', tt)
            sl = ib % 2
            vt = E('v', lambda v, pm=pm, sl=sl: v.tensor_copy(
                out=xn16[:, sl * T:(sl + 1) * T], in_=pm[:]), inc='vp')
            G['pm'][pmi] = vt
            W('q', vt)
            for jbb in range(4):
                tgt2 = D(w_cmb16s[jt * 4 + jbb, :, ib * P:(ib + 1) * P],
                         xn16[:, sl * T + jbb * P:sl * T + (jbb + 1) * P], 'qs')
            W('v', tgt2)

    W('t', pg.cur('vp'))
    W('t', pg.cur('gp'))
    W('t', pg.cur('qs'))
    W('s', pg.cur('vp'))
    W('g', pg.cur('vp'))
    W('v', pg.cur('tm'))
    W('v', pg.cur('gp'))
    W('q', pg.cur('vp'))

    ln2_t = None
    for c in range(NCH):
        t0 = c * T
        W('v', pg.cur('tm'))
        W('s', pg.cur('tp'))
        W('v', pg.cur('tp'))
        if c > 0:
            E('v', lambda v: v.tensor_copy(
                out=stage[:].rearrange("p (db u) -> p db u", u=T + 3)[:, :, 0:3],
                in_=tails[:].rearrange("p (db u) -> p db u", u=3)), inc='vp')

        for half in range(2):
            W('q', pg.cur('vp'))
            W('q', pg.cur('sp'))
            W('q', pg.cur('qo'))
            for i in range(2):
                ti = half * 2 + i
                xt = D(xbuf[:, i * DM:(i + 1) * DM], xin[t0 + ti * P:t0 + (ti + 1) * P, :], 'qx')
            W('v', xt)
            W('s', xt)
            for i in range(2):
                ti = half * 2 + i
                E('v', lambda v, i=i, ti=ti: v.tensor_reduce(
                    out=stats[:, 16 + ti:17 + ti], in_=xbuf[:, i * DM:(i + 1) * DM], axis=AX, op=ALU.add),
                    inc='vp' if i == 1 else None)
                E('s', lambda s, i=i, ti=ti: s.activation(
                    out=xn16[:, ti * DM:(ti + 1) * DM], in_=xbuf[:, i * DM:(i + 1) * DM],
                    func=AF.Square, accum_out=stats[:, 20 + ti:21 + ti]))
                E('s', lambda s, i=i, ti=ti: s.activation(
                    out=xn16[:, ti * DM:(ti + 1) * DM], in_=xbuf[:, i * DM:(i + 1) * DM],
                    func=AF.Copy), inc='sp' if i == 1 else None)
        W('s', pg.cur('vp'))
        E('s', lambda s: s.activation(out=stats[:, 24:28], in_=stats[:, 16:20],
                                      func=AF.Copy, scale=1.0 / DM))
        E('s', lambda s: s.activation(out=stats[:, 28:32], in_=stats[:, 20:24],
                                      func=AF.Copy, scale=1.0 / DM))
        st_ = E('s', lambda s: s.activation(out=stats[:, 16:20], in_=stats[:, 24:28],
                                            func=AF.Square), inc='sp')
        W('v', st_)
        vt = E('v', lambda v: v.scalar_tensor_tensor(
            out=stats[:, 20:24], in0=stats[:, 28:32], scalar=1e-5, in1=stats[:, 16:20],
            op0=ALU.add, op1=ALU.subtract), inc='vp')
        W('s', vt)
        E('s', lambda s: s.activation(out=stats[:, 20:24], in_=stats[:, 20:24], func=AF.Ln))
        st_ = E('s', lambda s: s.activation(out=stats[:, 20:24], in_=stats[:, 20:24],
                                            func=AF.Exp, scale=nhalf[:]), inc='sp')
        W('v', st_)
        for ti in range(4):
            E('v', lambda v, ti=ti: v.tensor_scalar(
                out=xn16[:, ti * DM:(ti + 1) * DM], in0=xn16[:, ti * DM:(ti + 1) * DM],
                scalar1=stats[:, 24 + ti:25 + ti], scalar2=stats[:, 20 + ti:21 + ti],
                op0=ALU.subtract, op1=ALU.mult), inc='vp' if ti == 3 else None)
        W('t', pg.cur('vp'))
        for ti in range(4):
            transpose_block(xn16[:, ti * DM:(ti + 1) * DM], MB,
                            (lambda ti: lambda v, mb, ps: v.tensor_copy(
                                out=xnt[:, mb * T + ti * P:mb * T + (ti + 1) * P], in_=ps))(ti),
                            ti * MB)
        W('t', pg.cur('vp'))
        W('q', pg.cur('tp'))
        W('q', pg.cur('vp'))
        W('s', pg.cur('gp'))
        for mpair in range(16):
            sl = mpair % 2
            so = sl * 2 * MB * P
            W('q', pg.cur('tm'))
            wt = D(winr[:, so:so + 2 * MB * P], w_in16s[mpair], 'qw')
            W('t', wt)
            for m in range(2):
                mb = 2 * mpair + m
                pmi = mb % 2
                tt, pm = mm_chain(pmi,
                                  lambda k, so=so, m=m: winr[:, so + (k * 2 + m) * P:so + (k * 2 + m + 1) * P],
                                  lambda k: xnt[:, k * T:(k + 1) * T], MB)
                if mb < DB:
                    W('v', tt)
                    G['pm'][pmi] = E('v', lambda v, pm=pm, mb=mb: v.tensor_scalar(
                        out=stage[:, mb * (T + 3) + 3:mb * (T + 3) + 3 + T], in0=pm[:],
                        scalar1=binc[:, mb:mb + 1], scalar2=None, op0=ALU.add), inc='vp')
                else:
                    W('s', tt)
                    G['pm'][pmi] = E('s', lambda s, pm=pm, mb=mb: s.activation(
                        out=srA[:, (mb - DB) * T:(mb - DB + 1) * T], in_=pm[:],
                        func=AF.Silu, bias=binc[:, mb:mb + 1]), inc='sp')
        W('t', pg.cur('vp'))
        W('t', pg.cur('sp'))

        for db in range(DB):
            pmi = db % 2
            so = db * (T + 3)
            half8 = (db % 2) * 4
            W('v', pg.cur('tm'))
            dt_ = None
            for tap in range(4):
                dt_ = E('v', lambda v, tap=tap, db=db, half8=half8: v.tensor_scalar(
                    out=diag[:, (half8 + tap) * P:(half8 + tap + 1) * P], in0=id16[:],
                    scalar1=wcv[:, db * 4 + tap:db * 4 + tap + 1], scalar2=None, op0=ALU.mult),
                    inc='vp' if tap == 3 else None)
            tt, pm = mm_chain(pmi,
                              lambda k, half8=half8: diag[:, (half8 + k) * P:(half8 + k + 1) * P],
                              lambda k, so=so: stage[:, so + k:so + k + T], 4, extra_wait=dt_)
            W('v', tt)
            G['pm'][pmi] = E('v', lambda v, pm=pm, db=db: v.tensor_scalar(
                out=spkA[:, db * T:(db + 1) * T], in0=pm[:],
                scalar1=thr[:, db:db + 1], scalar2=None, op0=ALU.is_ge), inc='vp')

        tl_t = E('v', lambda v: v.tensor_copy(
            out=tails[:].rearrange("p (db u) -> p db u", u=3),
            in_=stage[:].rearrange("p (db u) -> p db u", u=T + 3)[:, :, T:T + 3]), inc='vp')

        W('t', pg.cur('vp'))
        W('t', G['pm'][0])
        pm = pm0
        for k in range(DB):
            E('t', lambda t, k=k, pm=pm: t.matmul(
                out=pm[0:32, 0:T], lhsT=wxbcr[:, k * 32:(k + 1) * 32],
                rhs=spkA[:, k * T:(k + 1) * T], start=(k == 0), stop=False))
        tt = E('t', lambda t, pm=pm: t.matmul(
            out=pm[0:32, 0:T], lhsT=bxbr[:], rhs=ones_t[:], start=False, stop=True), inc='tm')
        W('v', tt)
        G['pm'][0] = E('v', lambda v, pm=pm: v.tensor_copy(
            out=bcb[:], in_=pm[0:32, :]), inc='vp')

        for jb in range(DB):
            sl = jb % 2
            so = sl * DB * P
            W('q', pg.cur('tm'))
            wt = D(wcmr[:, so:so + DB * P], w_cmb16s[jb], 'qw')
            pmi = jb % 2
            tt, pm = mm_chain(pmi,
                              lambda k, so=so: wcmr[:, so + k * P:so + (k + 1) * P],
                              lambda k: spkA[:, k * T:(k + 1) * T], DB, extra_wait=wt)
            W('s', tt)
            E('s', lambda s, pm=pm, jb=jb: s.activation(
                out=tmpa[:, (jb % 2) * T:(jb % 2 + 1) * T], in_=pm[:],
                func=AF.Exp, bias=bcmb[:, jb:jb + 1]))
            G['pm'][pmi] = E('s', lambda s, jb=jb: s.activation(
                out=d2A[:, jb * T:(jb + 1) * T], in_=tmpa[:, (jb % 2) * T:(jb % 2 + 1) * T],
                func=AF.Ln, bias=1.0), inc='sp')
        W('g', pg.cur('sp'))

        for db in range(DB):
            E('g', lambda g_, db=db: g_.tensor_tensor(
                out=wbA[:, db * T:(db + 1) * T], in0=d2A[:, db * T:(db + 1) * T],
                in1=spkA[:, db * T:(db + 1) * T], op=ALU.mult), inc='gp' if db == DB - 1 else None)

        wcmb_t = pg.cur('tm')

        W('v', pg.cur('gp'))
        W('t', pg.cur('vp'))
        grp_vp = {}
        hch = {}
        for s_i in range(DS):
            soh = (s_i % 2) * 2 * MB * P
            W('q', hch[s_i - 2] if s_i >= 2 else None)
            wt_h = D(w1r[:, soh:soh + 2 * MB * P], w1s[s_i], 'qw')
            if s_i >= 1:
                W('v', pg.cur('gp'))
            for which, dst in [(0, brep), (1, crep)]:
                pri = which
                pr = (pr0, pr1)[pri]
                row = s_i + which * DS
                W('t', G['pr'][pri])
                tt = E('t', lambda t, pr=pr, row=row: t.matmul(
                    out=pr[:], lhsT=ind32[:, row * P:(row + 1) * P], rhs=bcb[:],
                    start=True, stop=True), inc='tm')
                W('v', tt)
                G['pr'][pri] = E('v', lambda v, pr=pr, dst=dst, s_i=s_i: v.tensor_copy(
                    out=dst[:, (s_i % 2) * T:(s_i % 2 + 1) * T], in_=pr[:]), inc='vp')
            rep_t = G['pr'][0]
            rsl = (s_i % 2) * T
            h_info = []
            for m in range(2):
                hb = 2 * s_i + m
                pbi = hb % 2
                tt_h, pbt = mm_chain(pbi,
                                     lambda k, soh=soh, m=m: w1r[:, soh + (k * 2 + m) * P:soh + (k * 2 + m + 1) * P],
                                     lambda k: xnt[:, k * T:(k + 1) * T], MB,
                                     extra_wait=wt_h if m == 0 else None, pair='pb')
                hch[s_i] = tt_h
                h_info.append((hb, pbi, tt_h, pbt))
            for db in range(DB):
                g = s_i * DB + db
                so = (g % 2) * T
                soa = (g % 4) * T
                if g >= 4:
                    W('s', grp_vp[g - 4])
                at = E('s', lambda s, soa=soa, db=db, s_i=s_i: s.activation(
                    out=abar[:, soa:soa + T], in_=d2A[:, db * T:(db + 1) * T],
                    func=AF.Exp, scale=acol[:, db * DS + s_i:db * DS + s_i + 1]), inc='sp')
                if db == 0:
                    W('g', rep_t)
                if g >= 4:
                    W('g', grp_vp[g - 4])
                bt = E('g', lambda g_, soa=soa, db=db, rsl=rsl: g_.tensor_tensor(
                    out=bxs[:, soa:soa + T], in0=wbA[:, db * T:(db + 1) * T],
                    in1=brep[:, rsl:rsl + T], op=ALU.mult), inc='gp')
                W('v', at)
                W('v', bt)
                E('v', lambda v, so=so, soa=soa, g=g: v.tensor_tensor_scan(
                    out=hsc[:, so:so + T], data0=abar[:, soa:soa + T], data1=bxs[:, soa:soa + T],
                    initial=hstate[:, g:g + 1], op0=ALU.mult, op1=ALU.add))
                if s_i == 0:
                    E('v', lambda v, so=so, db=db, rsl=rsl: v.tensor_tensor(
                        out=yac[:, db * T:(db + 1) * T], in0=hsc[:, so:so + T],
                        in1=crep[:, rsl:rsl + T], op=ALU.mult))
                else:
                    E('v', lambda v, so=so, rsl=rsl: v.tensor_tensor(
                        out=hcs[:, so:so + T], in0=hsc[:, so:so + T],
                        in1=crep[:, rsl:rsl + T], op=ALU.mult))
                    E('v', lambda v, so=so, db=db: v.tensor_tensor(
                        out=yac[:, db * T:(db + 1) * T], in0=yac[:, db * T:(db + 1) * T],
                        in1=hcs[:, so:so + T], op=ALU.add))
                grp_vp[g] = E('v', lambda v, so=so, g=g: v.tensor_copy(
                    out=hstate[:, g:g + 1], in_=hsc[:, so + T - 1:so + T]), inc='vp')
            for hb, pbi, tt_h, pbt in h_info:
                W('s', tt_h)
                if hb == 0:
                    W('s', tl_t)
                G['pb'][pbi] = E('s', lambda s, pbt=pbt, hb=hb: s.activation(
                    out=hqb[:, hb * T:(hb + 1) * T], in_=pbt[:], func=AF.Relu,
                    bias=b1c[:, hb:hb + 1]), inc='sp')

        W('t', pg.cur('sp'))
        w2ch = {}
        for mb in range(MB):
            sl = mb % 2
            so = sl * HB * P
            W('q', w2ch[mb - 2] if mb >= 2 else wcmb_t)
            wt = D(w2r[:, so:so + HB * P], w2s[mb], 'qw')
            pbi = mb % 2
            tt, pb = mm_chain(pbi,
                              lambda k, so=so: w2r[:, so + k * P:so + (k + 1) * P],
                              lambda k: hqb[:, k * T:(k + 1) * T], HB,
                              extra_wait=wt, pair='pb')
            w2ch[mb] = tt
            W('v', tt)
            vt = E('v', lambda v, pb=pb, mb=mb: v.tensor_scalar(
                out=oqb[:, mb * T:(mb + 1) * T], in0=pb[:], scalar1=b2qc[:, mb:mb + 1],
                scalar2=None, op0=ALU.add), inc='vp')
            G['pb'][pbi] = vt
            W('s', vt)
            E('s', lambda s, mb=mb: s.activation(
                out=oqb[:, mb * T:(mb + 1) * T], in_=oqb[:, mb * T:(mb + 1) * T],
                func=AF.Silu), inc='sp')

        for db in range(DB):
            E('v', lambda v, db=db: v.scalar_tensor_tensor(
                out=yac[:, db * T:(db + 1) * T], in0=spkA[:, db * T:(db + 1) * T],
                scalar=dcol[:, db:db + 1], in1=yac[:, db * T:(db + 1) * T],
                op0=ALU.mult, op1=ALU.add), inc='vp' if db == DB - 1 else None)
        W('g', pg.cur('vp'))
        for db in range(DB):
            E('g', lambda g_, db=db: g_.tensor_tensor(
                out=wbA[:, db * T:(db + 1) * T], in0=yac[:, db * T:(db + 1) * T],
                in1=srA[:, db * T:(db + 1) * T], op=ALU.mult), inc='gp' if db == DB - 1 else None)
        W('t', pg.cur('gp'))

        for mb in range(MB):
            sl = mb % 2
            so = sl * DB * P
            W('q', pg.cur('tm'))
            wt = D(wor[:, so:so + DB * P], w_outs[mb], 'qw')
            pmi = mb % 2
            tt, pm = mm_chain(pmi,
                              lambda k, so=so: wor[:, so + k * P:so + (k + 1) * P],
                              lambda k: wbA[:, k * T:(k + 1) * T], DB, extra_wait=wt)
            W('v', tt)
            G['pm'][pmi] = E('v', lambda v, pm=pm, mb=mb: v.scalar_tensor_tensor(
                out=ob[:, mb * T:(mb + 1) * T], in0=pm[:], scalar=boutc[:, mb:mb + 1],
                in1=xnt[:, mb * T:(mb + 1) * T], op0=ALU.add, op1=ALU.add), inc='vp')

        W('s', pg.cur('vp'))
        W('s', ln2_t)
        W('t', G['pr'][0])
        W('t', G['pr'][1])
        sqdone = {}
        for mb in range(MB):
            if mb >= 2:
                W('s', sqdone[mb - 2])
            st_ = E('s', lambda s, mb=mb: s.activation(
                out=o2t[:, (mb % 2) * T:(mb % 2 + 1) * T], in_=ob[:, mb * T:(mb + 1) * T],
                func=AF.Square), inc='sp')
            W('t', st_)
            E('t', lambda t, mb=mb: t.matmul(
                out=pr0[0:1, :], lhsT=onek16[:], rhs=ob[:, mb * T:(mb + 1) * T],
                start=(mb == 0), stop=(mb == MB - 1)))
            sqdone[mb] = E('t', lambda t, mb=mb: t.matmul(
                out=pr1[0:1, :], lhsT=onek16[:], rhs=o2t[:, (mb % 2) * T:(mb % 2 + 1) * T],
                start=(mb == 0), stop=(mb == MB - 1)), inc='tm')
        ln2_t = pg.cur('tm')
        W('v', pg.cur('tm'))
        E('v', lambda v: v.tensor_scalar(out=st1[0:1, 0:T], in0=pr0[0:1, :], scalar1=1.0 / DM,
                                         scalar2=None, op0=ALU.mult))
        E('v', lambda v: v.tensor_scalar(out=st1[0:1, T:2 * T], in0=pr1[0:1, :], scalar1=1.0 / DM,
                                         scalar2=None, op0=ALU.mult), inc='vp')
        G['pr'][0] = G['pr'][1] = pg.cur('vp')
        E('v', lambda v: v.tensor_tensor(out=st2[0:1, 0:T], in0=st1[0:1, 0:T], in1=st1[0:1, 0:T],
                                         op=ALU.mult))
        E('v', lambda v: v.tensor_tensor(out=st1[0:1, T:2 * T], in0=st1[0:1, T:2 * T],
                                         in1=st2[0:1, 0:T], op=ALU.subtract))
        vt = E('v', lambda v: v.tensor_scalar(out=st1[0:1, T:2 * T], in0=st1[0:1, T:2 * T],
                                              scalar1=1e-5, scalar2=None, op0=ALU.add), inc='vp')
        W('s', vt)
        E('s', lambda s: s.activation(out=st1[0:1, T:2 * T], in_=st1[0:1, T:2 * T], func=AF.Ln))
        st_ = E('s', lambda s: s.activation(out=st1[0:1, T:2 * T], in_=st1[0:1, T:2 * T],
                                            func=AF.Exp, scale=nhalf[0:1, :]), inc='sp')
        W('v', st_)
        E('v', lambda v: v.tensor_copy(out=st2[0:1, 0:T], in_=st1[0:1, 0:T]))
        vt = E('v', lambda v: v.tensor_copy(out=st2[0:1, T:2 * T], in_=st1[0:1, T:2 * T]), inc='vp')
        W('t', vt)
        W('t', G['pb'][0])
        W('t', G['pb'][1])
        E('t', lambda t: t.matmul(out=pb0[:], lhsT=ones_r[:], rhs=st2[0:1, 0:T],
                                  start=True, stop=True))
        tt = E('t', lambda t: t.matmul(out=pb1[:], lhsT=ones_r[:], rhs=st2[0:1, T:2 * T],
                                       start=True, stop=True), inc='tm')
        W('v', tt)
        for mb in range(MB):
            E('v', lambda v, mb=mb: v.tensor_tensor(
                out=tmpa[:, 0:T], in0=ob[:, mb * T:(mb + 1) * T], in1=pb0[:], op=ALU.subtract))
            E('v', lambda v: v.tensor_tensor(
                out=tmpa[:, 0:T], in0=tmpa[:, 0:T], in1=pb1[:], op=ALU.mult))
            E('v', lambda v, mb=mb: v.tensor_scalar(
                out=tmpa[:, T:2 * T], in0=tmpa[:, 0:T], scalar1=gamc[:, mb:mb + 1],
                scalar2=betc[:, mb:mb + 1], op0=ALU.mult, op1=ALU.add))
            E('v', lambda v, mb=mb: v.tensor_scalar(
                out=sb16[:, mb * T:(mb + 1) * T], in0=tmpa[:, T:2 * T], scalar1=1.0,
                scalar2=None, op0=ALU.is_ge), inc='vp' if mb == MB - 1 else None)
        G['pb'][0] = G['pb'][1] = pg.cur('vp')

        W('v', pg.cur('sp'))
        for mb in range(MB):
            E('v', lambda v, mb=mb: v.tensor_tensor(
                out=oqb[:, mb * T:(mb + 1) * T], in0=oqb[:, mb * T:(mb + 1) * T],
                in1=sb16[:, mb * T:(mb + 1) * T], op=ALU.mult), inc='vp' if mb == MB - 1 else None)

        W('v', pg.cur('gp'))
        W('t', pg.cur('vp'))
        for mb in range(MB):
            transpose_block(oqb[:, mb * T:(mb + 1) * T], 4,
                            (lambda mb: lambda v, ti, ps: v.tensor_copy(
                                out=sgt[:, ti * DM + mb * P:ti * DM + (mb + 1) * P], in_=ps))(mb),
                            mb * 4)
        for half in range(2):
            W('q', pg.cur('qo'))
            for i in range(2):
                ti = half * 2 + i
                xt = D(xbuf[:, i * DM:(i + 1) * DM], xin[t0 + ti * P:t0 + (ti + 1) * P, :], 'qx')
            W('v', xt)
            ot = None
            for i in range(2):
                ti = half * 2 + i
                ot = E('v', lambda v, i=i, ti=ti: v.tensor_tensor(
                    out=xbuf[:, i * DM:(i + 1) * DM], in0=xbuf[:, i * DM:(i + 1) * DM],
                    in1=sgt[:, ti * DM:(ti + 1) * DM], op=ALU.add), inc='vp')
            W('q', ot)
            for i in range(2):
                ti = half * 2 + i
                D(yout[t0 + ti * P:t0 + (ti + 1) * P, :], xbuf[:, i * DM:(i + 1) * DM], 'qo')
            W('v', pg.cur('qo'))

    W('q', pg.cur('qo'))
    W('v', pg.cur('qo'))

    with nc.Block() as block:
        @block.sync
        def _(q):
            for eng, fn in pg.cmds:
                if eng == 'q':
                    fn(q)

        @block.vector
        def _(v):
            for eng, fn in pg.cmds:
                if eng == 'v':
                    fn(v)

        @block.scalar
        def _(s):
            for eng, fn in pg.cmds:
                if eng == 's':
                    fn(s)

        @block.tensor
        def _(t):
            for eng, fn in pg.cmds:
                if eng == 't':
                    fn(t)

        @block.gpsimd
        def _(g):
            for eng, fn in pg.cmds:
                if eng == 'g':
                    fn(g)

    return nc


_AUX = None


def _aux():
    global _AUX
    if _AUX is None:
        a = np.zeros((P, P + 1), np.float32)
        a[:, 0:P] = np.eye(P, dtype=np.float32)
        a[:, P] = 1.0
        _AUX = a
    return _AUX


def kernel(L0_attn_x, L0_attn_W_in, L0_attn_b_in, L0_attn_W_conv, L0_attn_b_conv,
           L0_attn_W_x, L0_attn_b_x, L0_attn_W_dt, L0_attn_b_dt, L0_attn_A_log,
           L0_attn_D_param, L0_attn_W_out, L0_attn_b_out, L0_attn_gamma, L0_attn_beta,
           L0_ffn_w1, L0_ffn_b1, L0_ffn_w2, L0_ffn_b2):
    nc = build_nc()
    base = {
        "w_in": np.ascontiguousarray(L0_attn_W_in, np.float32),
        "b_in": np.ascontiguousarray(L0_attn_b_in, np.float32),
        "w_conv": np.ascontiguousarray(np.asarray(L0_attn_W_conv).reshape(DI, 4), np.float32),
        "b_conv": np.ascontiguousarray(L0_attn_b_conv, np.float32),
        "w_x": np.ascontiguousarray(L0_attn_W_x, np.float32),
        "b_x": np.ascontiguousarray(L0_attn_b_x, np.float32),
        "w_dt": np.ascontiguousarray(L0_attn_W_dt, np.float32),
        "b_dt": np.ascontiguousarray(L0_attn_b_dt, np.float32),
        "a_log": np.ascontiguousarray(L0_attn_A_log, np.float32),
        "d_par": np.ascontiguousarray(L0_attn_D_param, np.float32),
        "w_out": np.ascontiguousarray(L0_attn_W_out, np.float32),
        "b_out": np.ascontiguousarray(L0_attn_b_out, np.float32),
        "gam": np.ascontiguousarray(L0_attn_gamma, np.float32),
        "bet": np.ascontiguousarray(L0_attn_beta, np.float32),
        "w1": np.ascontiguousarray(L0_ffn_w1, np.float32),
        "b1": np.ascontiguousarray(L0_ffn_b1, np.float32),
        "w2": np.ascontiguousarray(L0_ffn_w2, np.float32),
        "b2": np.ascontiguousarray(L0_ffn_b2, np.float32),
        "aux": _aux(),
    }
    x = np.ascontiguousarray(L0_attn_x, np.float32)
    in_maps = [dict(base, x=np.ascontiguousarray(x[b])) for b in range(8)]
    res = run_bass_kernel_spmd(nc, in_maps, core_ids=list(range(8)))
    return np.stack([np.asarray(r["out"], np.float32) for r in res.results], axis=0)


# version 11
# speedup vs baseline: 1.0445x; 1.0445x over previous
import numpy as np
import concourse.bass as bass
import concourse.mybir as mybir
from concourse.bass_utils import run_bass_kernel_spmd

F32 = mybir.dt.float32
F16 = mybir.dt.float16
F8 = mybir.dt.float8e4
AF = mybir.ActivationFunctionType
ALU = mybir.AluOpType
AX = mybir.AxisListType.X

P = 128
T = 512
NCH = 4096 // T
DM = 1024
DI = 2048
DS = 16
HF = 4096
DB = DI // P
MB = DM // P
HB = HF // P


class Prog:
    def __init__(self, nc):
        self.nc = nc
        self.cmds = []
        self.sems = {}

    def _sem(self, name):
        if name not in self.sems:
            self.sems[name] = [self.nc.alloc_semaphore(name), 0]
        return self.sems[name]

    def emit(self, eng, fn, inc=None, by=1):
        if inc is None:
            self.cmds.append((eng, fn))
            return None
        rec = self._sem(inc)
        rec[1] += by
        tgt, h = rec[1], rec[0]

        def wrapped(e, fn=fn, h=h, by=by):
            fn(e).then_inc(h, by)
        self.cmds.append((eng, wrapped))
        return (inc, tgt)

    def dma(self, out, in_, inc='qw', eng='q', slow=False):
        rec = self._sem(inc)
        rec[1] += 16
        tgt, h = rec[1], rec[0]

        def fn(e, out=out, in_=in_, h=h, slow=slow):
            e.dma_start(out=out, in_=in_, allow_slow_non_contiguous=slow).then_inc(h, 16)
        self.cmds.append((eng, fn))
        return (inc, tgt)

    def wait(self, eng, name, val=None):
        if isinstance(name, tuple):
            name, val = name
        if name is None:
            return
        rec = self._sem(name)
        v = rec[1] if val is None else val
        if v > 0:
            self.cmds.append((eng, lambda e, h=rec[0], v=v: e.wait_ge(h, v)))

    def cur(self, name):
        return (name, self._sem(name)[1])


def build_nc():
    nc = bass.Bass("TRN2", target_bir_lowering=False, debug=False)

    xin = nc.dram_tensor("x", [4096, DM], F32, kind="ExternalInput").ap()
    w_in = nc.dram_tensor("w_in", [DM, 2 * DI], F32, kind="ExternalInput").ap()
    b_in = nc.dram_tensor("b_in", [2 * DI], F32, kind="ExternalInput").ap()
    w_conv = nc.dram_tensor("w_conv", [DI, 4], F32, kind="ExternalInput").ap()
    b_conv = nc.dram_tensor("b_conv", [DI], F32, kind="ExternalInput").ap()
    w_x = nc.dram_tensor("w_x", [DI, DI + 2 * DS], F32, kind="ExternalInput").ap()
    b_x = nc.dram_tensor("b_x", [DI + 2 * DS], F32, kind="ExternalInput").ap()
    w_dt = nc.dram_tensor("w_dt", [DI, DI], F32, kind="ExternalInput").ap()
    b_dt = nc.dram_tensor("b_dt", [DI], F32, kind="ExternalInput").ap()
    a_log = nc.dram_tensor("a_log", [DI, DS], F32, kind="ExternalInput").ap()
    d_par = nc.dram_tensor("d_par", [DI], F32, kind="ExternalInput").ap()
    w_out = nc.dram_tensor("w_out", [DI, DM], F32, kind="ExternalInput").ap()
    b_out = nc.dram_tensor("b_out", [DM], F32, kind="ExternalInput").ap()
    gam = nc.dram_tensor("gam", [DM], F32, kind="ExternalInput").ap()
    bet = nc.dram_tensor("bet", [DM], F32, kind="ExternalInput").ap()
    w1 = nc.dram_tensor("w1", [HF, DM], F32, kind="ExternalInput").ap()
    b1 = nc.dram_tensor("b1", [HF], F32, kind="ExternalInput").ap()
    w2 = nc.dram_tensor("w2", [DM, HF], F32, kind="ExternalInput").ap()
    b2 = nc.dram_tensor("b2", [DM], F32, kind="ExternalInput").ap()
    aux = nc.dram_tensor("aux", [P, P + 1], F32, kind="ExternalInput").ap()
    yout = nc.dram_tensor("out", [4096, DM], F32, kind="ExternalOutput").ap()

    w_in16s = nc.dram_tensor("w_in16s", [16, P, 2 * MB * P], F16, kind="Internal").ap()
    w_cmb16s = nc.dram_tensor("w_cmb16s", [DB, P, DB * P], F16, kind="Internal").ap()
    w_outs = nc.dram_tensor("w_outs", [MB, P, DB * P], F16, kind="Internal").ap()
    w1s = nc.dram_tensor("w1s", [DB, P, 2 * MB * P], F16, kind="Internal").ap()
    w2s = nc.dram_tensor("w2s", [MB, P, HB * P], F8, kind="Internal").ap()
    wdt16 = nc.dram_tensor("wdt16", [DI, DI], F16, kind="Internal").ap()
    wxd16 = nc.dram_tensor("wxd16", [DI, DI], F16, kind="Internal").ap()

    A = nc.alloc_sbuf_tensor
    id16 = A("id16", [P, P], F16)
    idf32 = A("idf32", [P, P], F32)
    ones_r = A("ones_r", [1, P], F16)
    ones_t = A("ones_t", [1, T], F16)
    onek16 = A("onek16", [P, 1], F16)
    nhalf = A("nhalf", [P, 1], F32)
    ind32 = A("ind32", [32, 32 * P], F16)
    acol = A("acol", [P, DB * DS], F32)
    dcol = A("dcol", [P, DB], F32)
    thr = A("thr", [P, DB], F32)
    binc = A("binc", [P, 2 * DB], F32)
    bcmb = A("bcmb", [P, DB], F32)
    bdtc = A("bdtc", [P, DB], F32)
    bxbr = A("bxbr", [1, 32], F16)
    bxdc = A("bxdc", [P, DB], F16)
    boutc = A("boutc", [P, MB], F32)
    gamc = A("gamc", [P, MB], F32)
    betc = A("betc", [P, MB], F32)
    b1c = A("b1c", [P, HB], F32)
    b2q16 = A("b2q16", [P, MB], F16)
    b2qc = A("b2qc", [P, MB], F32)
    wcv = A("wcv", [P, 4 * DB], F32)
    hstate = A("hstate", [P, DB * DS], F32)
    stats = A("stats", [P, 32], F32)
    tails = A("tails", [P, DB * 3], F16)
    diag = A("diag", [P, 8 * P], F16)

    xbuf = A("xbuf", [P, 2 * DM], F32)
    xn16 = A("xn16", [P, 4 * DM], F16)
    xnt = A("xnt", [P, MB * T], F16)
    stage = A("stage", [P, DB * (T + 3)], F16)
    wbig = A("wbig", [P, 4 * DB * T], F16)
    srA = wbig[:, 0 * DB * T:1 * DB * T]
    spkA = wbig[:, 1 * DB * T:2 * DB * T]
    d2A = wbig[:, 2 * DB * T:3 * DB * T]
    wbA = wbig[:, 3 * DB * T:4 * DB * T]
    bcb = A("bcb", [32, T], F16)
    brep = A("brep", [P, 2 * T], F16)
    crep = A("crep", [P, 2 * T], F16)
    abar = A("abar", [P, 4 * T], F16)
    bxs = A("bxs", [P, 4 * T], F16)
    hsc = A("hsc", [P, 2 * T], F16)
    hcs = A("hcs", [P, 2 * T], F16)
    yac = A("yac", [P, DB * T], F16)
    ob = A("ob", [P, MB * T], F16)
    o2t = A("o2t", [P, 2 * T], F16)
    sb16 = A("sb16", [P, MB * T], F16)
    st1 = A("st1", [1, 2 * T], F32)
    st2 = A("st2", [1, 2 * T], F16)
    tmpa = A("tmpa", [P, 2 * T], F16)
    winr = A("winr", [P, 2 * 2 * MB * P], F16)
    wcmr = A("wcmr", [P, 2 * DB * P], F16)
    wor = A("wor", [P, 2 * DB * P], F16)
    wxbcr = A("wxbcr", [P, DB * 32], F16)

    w1r = winr
    oqb = winr
    sgt = yac
    rhs_strip = yac
    hqb = stage[:, 0:HB * T // 2].bitcast(F8)
    w2r = wcmr[:].bitcast(F8)
    hq8s = sb16[:].bitcast(F8)

    pm0 = nc.alloc_psum_tensor("pm0", [P, T], F32)
    pm1 = nc.alloc_psum_tensor("pm1", [P, T], F32)
    pr0 = nc.alloc_psum_tensor("pr0", [P, T], F32)
    pr1 = nc.alloc_psum_tensor("pr1", [P, T], F32)
    pb0 = nc.alloc_psum_tensor("pb0", [P, T], F32)
    pb1 = nc.alloc_psum_tensor("pb1", [P, T], F32)
    pt0 = nc.alloc_psum_tensor("pt0", [P, 1024], F16)
    pt1 = nc.alloc_psum_tensor("pt1", [P, 1024], F16)

    pg = Prog(nc)
    E, W, D = pg.emit, pg.wait, pg.dma

    NOG = (None, 0)
    G = {'pm': [NOG, NOG], 'pr': [NOG, NOG], 'pb': [NOG, NOG], 'pt': [NOG, NOG]}

    def mm_chain(pmi, lhs_fn, rhs_fn, n, nfree=T, rows=None, extra_wait=None, pair='pm'):
        pm = (pm0, pm1)[pmi] if pair == 'pm' else (pb0, pb1)[pmi]
        W('t', G[pair][pmi])
        if extra_wait is not None:
            W('t', extra_wait)
        tt = None
        for k in range(n):
            tt = E('t', lambda t, k=k, pm=pm: t.matmul(
                out=pm[:, 0:nfree] if rows is None else pm[0:rows, 0:nfree],
                lhsT=lhs_fn(k), rhs=rhs_fn(k),
                start=(k == 0), stop=(k == n - 1)), inc='tm' if k == n - 1 else None)
        return tt, pm

    tgt = D(xbuf[:, 0:P + 1], aux, 'qx')
    W('v', tgt)
    E('v', lambda v: v.tensor_copy(out=id16[:], in_=xbuf[:, 0:P]))
    E('v', lambda v: v.tensor_copy(out=idf32[:], in_=xbuf[:, 0:P]))
    E('v', lambda v: v.tensor_copy(out=onek16[:], in_=xbuf[:, P:P + 1]))
    E('v', lambda v: v.tensor_copy(out=ones_r[:], in_=xbuf[0:1, P:P + 1].broadcast_to([1, P])))
    E('v', lambda v: v.tensor_copy(out=ones_t[:], in_=xbuf[0:1, P:P + 1].broadcast_to([1, T])))
    E('v', lambda v: v.memset(nhalf[:], -0.5))
    E('v', lambda v: v.memset(hstate[:], 0.0))
    E('v', lambda v: v.memset(stage[:], 0.0))
    vt = E('v', lambda v: v.memset(ind32[:], 1.0), inc='vp')
    W('g', vt)
    for s_i in range(32):
        E('g', lambda g_, s_i=s_i: g_.affine_select(
            out=ind32[:, s_i * P:(s_i + 1) * P], in_=ind32[:, s_i * P:(s_i + 1) * P],
            pattern=[[0, P]], compare_op=ALU.is_equal, fill=0.0,
            base=-s_i, channel_multiplier=1), inc='gp' if s_i == 31 else None)

    W('t', pg.cur('vp'))

    def cv(src2d, nb, sink):
        W('q', pg.cur('tp'))
        tgt = D(xbuf[0:nb, DM:DM + P], src2d, 'qx')
        W('t', tgt)
        W('t', G['pb'][0])
        tt = E('t', lambda t: t.transpose(
            out=pb0[:, 0:P], in_=xbuf[:, DM:DM + P], identity=idf32[:]), inc='tp')
        W('v', tt)
        G['pb'][0] = E('v', sink, inc='vp')

    cv(b_conv.rearrange("(a b) -> a b", b=P), DB,
       lambda v: v.tensor_scalar(out=thr[:], in0=pb0[:, 0:DB], scalar1=-1.0, scalar2=1.0,
                                 op0=ALU.mult, op1=ALU.add))
    cv(d_par.rearrange("(a b) -> a b", b=P), DB,
       lambda v: v.tensor_copy(out=dcol[:], in_=pb0[:, 0:DB]))
    cv(b_in.rearrange("(a b) -> a b", b=P), 2 * DB,
       lambda v: v.tensor_copy(out=binc[:], in_=pb0[:, 0:2 * DB]))
    cv(b_out.rearrange("(a b) -> a b", b=P), MB,
       lambda v: v.tensor_copy(out=boutc[:], in_=pb0[:, 0:MB]))
    cv(gam.rearrange("(a b) -> a b", b=P), MB,
       lambda v: v.tensor_copy(out=gamc[:], in_=pb0[:, 0:MB]))
    cv(bet.rearrange("(a b) -> a b", b=P), MB,
       lambda v: v.tensor_copy(out=betc[:], in_=pb0[:, 0:MB]))
    cv(b1.rearrange("(a b) -> a b", b=P), HB,
       lambda v: v.tensor_copy(out=b1c[:], in_=pb0[:, 0:HB]))
    cv(b2.rearrange("(a b) -> a b", b=P), MB,
       lambda v: v.tensor_copy(out=b2q16[:], in_=pb0[:, 0:MB]))
    W('s', pg.cur('vp'))
    E('s', lambda s: s.activation(out=b2qc[:], in_=b2q16[:], func=AF.Copy), inc='sp')
    cv(b_x[0:DI].rearrange("(a b) -> a b", b=P), DB,
       lambda v: v.tensor_copy(out=bxdc[:], in_=pb0[:, 0:DB]))
    cv(b_dt.rearrange("(a b) -> a b", b=P), DB,
       lambda v: v.tensor_copy(out=bdtc[:], in_=pb0[:, 0:DB]))
    tgt = D(wcv[:].rearrange("p (db t) -> p db t", t=4),
            w_conv.rearrange("(db p) t -> p db t", p=P), 'qx')
    tgt = D(xbuf[0:1, 0:32], b_x[DI:DI + 2 * DS].rearrange("(a b) -> a b", a=1), 'qx')
    W('v', tgt)
    E('v', lambda v: v.tensor_copy(out=bxbr[:], in_=xbuf[0:1, 0:32]), inc='vp')
    for db in range(DB):
        tgt = D(xbuf[:, DM + 256 + db * DS:DM + 256 + (db + 1) * DS],
                a_log[db * P:(db + 1) * P, :], 'qx')
    W('s', pg.cur('qx'))
    st_ = E('s', lambda s: s.activation(out=xbuf[:, DM + 512:DM + 512 + DB * DS],
                                        in_=xbuf[:, DM + 256:DM + 256 + DB * DS], func=AF.Exp), inc='sp')
    W('v', st_)
    E('v', lambda v: v.tensor_scalar(out=acol[:], in0=xbuf[:, DM + 512:DM + 512 + DB * DS],
                                     scalar1=-1.0, scalar2=None, op0=ALU.mult))
    E('v', lambda v: v.engine_nop(), inc='vp')

    for i in range(4):
        D(wdt16[i * 512:(i + 1) * 512, :], w_dt[i * 512:(i + 1) * 512, :], 'qc', eng='g')
    for i in range(4):
        D(wxd16[i * 512:(i + 1) * 512, :], w_x[i * 512:(i + 1) * 512, 0:DI], 'qc', eng='g')
    D(wxbcr[:].rearrange("p (rb m) -> p rb m", m=32),
      w_x[:, DI:DI + 32].rearrange("(rb p) m -> p rb m", p=P), 'qc', eng='g')
    qc_wdt = pg.cur('qc')
    for mp in range(16):
        D(w_in16s[mp].rearrange("p (rb m) -> p rb m", m=256),
          w_in[:, mp * 256:(mp + 1) * 256].rearrange("(rb p) m -> p rb m", p=P), 'qc', eng='g')
    for mb in range(MB):
        D(w_outs[mb].rearrange("p (rb m) -> p rb m", m=P),
          w_out[:, mb * P:(mb + 1) * P].rearrange("(rb p) m -> p rb m", p=P), 'qc', eng='g')

    def transpose_block(src_f16, n_tiles, sink, base):
        j = 0
        last = None
        while j < n_tiles:
            batch = min(8, n_tiles - j)
            pti = ((base + j) // 8) % 2
            pt = (pt0, pt1)[pti]
            W('t', G['pt'][pti])
            for jj in range(batch):
                tt = E('t', lambda t, jj=jj, j=j, pt=pt, src=src_f16: t.transpose(
                    out=pt[:, jj * P:(jj + 1) * P], in_=src[:, (j + jj) * P:(j + jj + 1) * P],
                    identity=id16[:]), inc='tp' if jj == batch - 1 else None)
            W('v', tt)
            for jj in range(batch):
                vt = E('v', lambda v, jj=jj, j=j, pt=pt: sink(v, j + jj, pt[:, jj * P:(jj + 1) * P]),
                       inc='vp' if jj == batch - 1 else None)
            G['pt'][pti] = vt
            last = vt
            j += batch
        return last

    for hb in range(HB):
        half = (hb % 2) * DM
        W('q', pg.cur('vp'))
        lt = D(xbuf[:, 0:DM], w1[hb * P:(hb + 1) * P, :], 'qw')
        W('v', lt)
        vt = E('v', lambda v, half=half: v.tensor_copy(
            out=xn16[:, half:half + DM], in_=xbuf[:, 0:DM]), inc='vp')
        W('t', vt)
        transpose_block(xn16[:, half:half + DM], MB,
                        (lambda hb: lambda v, kb, ps: v.tensor_copy(
                            out=wbig[:, kb * HF + hb * P:kb * HF + (hb + 1) * P], in_=ps))(hb),
                        hb * MB)
    W('q', pg.cur('vp'))
    for kb in range(MB):
        for mp in range(DB):
            D(w1s[mp, :, kb * 256:(kb + 1) * 256],
              wbig[:, kb * HF + mp * 256:kb * HF + (mp + 1) * 256], 'qs')
    W('v', pg.cur('qs'))

    w2cp = {}
    for rb in range(MB):
        h8 = (rb % 2) * HF
        if rb >= 2:
            W('g', w2cp[rb - 2])
        l8 = D(hq8s[:, h8:h8 + HF], w2[rb * P:(rb + 1) * P, :], 'qh2', eng='g')
        W('v', l8)
        W('v', pg.cur('tp'))
        vt = E('v', lambda v, h8=h8: v.tensor_copy(out=xn16[:, 0:HF], in_=hq8s[:, h8:h8 + HF]), inc='vp')
        w2cp[rb] = vt
        W('t', vt)
        transpose_block(xn16[:, 0:HF], HB,
                        (lambda rb: lambda v, hb, ps: v.tensor_copy(
                            out=wbig[:, (hb * MB + rb) * P:(hb * MB + rb + 1) * P], in_=ps))(rb),
                        rb * HB)
    for hb in range(HB):
        vt = E('v', lambda v, hb=hb: v.tensor_copy(
            out=hq8s[:, 0:DM], in_=wbig[:, hb * MB * P:(hb + 1) * MB * P]), inc='vp')
        W('q', vt)
        for mb in range(MB):
            tgt2 = D(w2s[mb, :, hb * P:(hb + 1) * P], hq8s[:, mb * P:(mb + 1) * P], 'qs')
        W('v', tgt2)

    W('q', qc_wdt)
    for ib in range(DB):
        W('q', pg.cur('tp'))
        tgt = D(xn16[:, 0:DI], wxd16[ib * P:(ib + 1) * P, :], 'qw')
        W('t', tgt)
        transpose_block(xn16[:, 0:DI], DB,
                        (lambda ib: lambda v, kb, ps: v.tensor_copy(
                            out=wbig[:, (kb * DB + ib) * P:(kb * DB + ib + 1) * P], in_=ps))(ib),
                        ib * DB)

    for jt in range(DI // T):
        W('q', pg.cur('tm'))
        for kb in range(DB):
            tgt = D(rhs_strip[:, kb * T:(kb + 1) * T],
                    wdt16[kb * P:(kb + 1) * P, jt * T:(jt + 1) * T], 'qw')
        W('t', tgt)
        for jb_in in range(4):
            jb = jt * 4 + jb_in
            pmi = jb % 2
            tt, pm = mm_chain(pmi,
                              lambda k, jb_in=jb_in: rhs_strip[:, k * T + jb_in * P:k * T + jb_in * P + P],
                              lambda k: bxdc[:, k:k + 1], DB, nfree=1)
            W('v', tt)
            G['pm'][pmi] = E('v', lambda v, jb=jb, pm=pm: v.tensor_tensor(
                out=bcmb[:, jb:jb + 1], in0=pm[:, 0:1], in1=bdtc[:, jb:jb + 1], op=ALU.add), inc='vp')
        for ib in range(DB):
            pmi = ib % 2
            tt, pm = mm_chain(pmi,
                              lambda k, ib=ib: wbig[:, (k * DB + ib) * P:(k * DB + ib + 1) * P],
                              lambda k: rhs_strip[:, k * T:(k + 1) * T], DB)
            W('v', tt)
            sl = ib % 2
            vt = E('v', lambda v, pm=pm, sl=sl: v.tensor_copy(
                out=xn16[:, sl * T:(sl + 1) * T], in_=pm[:]), inc='vp')
            G['pm'][pmi] = vt
            W('q', vt)
            for jbb in range(4):
                tgt2 = D(w_cmb16s[jt * 4 + jbb, :, ib * P:(ib + 1) * P],
                         xn16[:, sl * T + jbb * P:sl * T + (jbb + 1) * P], 'qs')
            W('v', tgt2)

    W('q', pg.cur('qc'))
    W('t', pg.cur('qc'))
    W('t', pg.cur('vp'))
    W('t', pg.cur('gp'))
    W('t', pg.cur('qs'))
    W('s', pg.cur('vp'))
    W('g', pg.cur('vp'))
    W('v', pg.cur('tm'))
    W('v', pg.cur('gp'))
    W('q', pg.cur('vp'))

    ln2_t = None
    for c in range(NCH):
        t0 = c * T
        W('v', pg.cur('tm'))
        W('s', pg.cur('tp'))
        W('v', pg.cur('tp'))
        if c > 0:
            E('v', lambda v: v.tensor_copy(
                out=stage[:].rearrange("p (db u) -> p db u", u=T + 3)[:, :, 0:3],
                in_=tails[:].rearrange("p (db u) -> p db u", u=3)), inc='vp')

        for half in range(2):
            W('q', pg.cur('vp'))
            W('q', pg.cur('sp'))
            W('q', pg.cur('qo'))
            for i in range(2):
                ti = half * 2 + i
                xt = D(xbuf[:, i * DM:(i + 1) * DM], xin[t0 + ti * P:t0 + (ti + 1) * P, :], 'qx')
            W('v', xt)
            W('s', xt)
            for i in range(2):
                ti = half * 2 + i
                E('v', lambda v, i=i, ti=ti: v.tensor_reduce(
                    out=stats[:, 16 + ti:17 + ti], in_=xbuf[:, i * DM:(i + 1) * DM], axis=AX, op=ALU.add),
                    inc='vp' if i == 1 else None)
                E('s', lambda s, i=i, ti=ti: s.activation(
                    out=xn16[:, ti * DM:(ti + 1) * DM], in_=xbuf[:, i * DM:(i + 1) * DM],
                    func=AF.Square, accum_out=stats[:, 20 + ti:21 + ti]))
                E('s', lambda s, i=i, ti=ti: s.activation(
                    out=xn16[:, ti * DM:(ti + 1) * DM], in_=xbuf[:, i * DM:(i + 1) * DM],
                    func=AF.Copy), inc='sp' if i == 1 else None)
        W('s', pg.cur('vp'))
        E('s', lambda s: s.activation(out=stats[:, 24:28], in_=stats[:, 16:20],
                                      func=AF.Copy, scale=1.0 / DM))
        E('s', lambda s: s.activation(out=stats[:, 28:32], in_=stats[:, 20:24],
                                      func=AF.Copy, scale=1.0 / DM))
        st_ = E('s', lambda s: s.activation(out=stats[:, 16:20], in_=stats[:, 24:28],
                                            func=AF.Square), inc='sp')
        W('v', st_)
        vt = E('v', lambda v: v.scalar_tensor_tensor(
            out=stats[:, 20:24], in0=stats[:, 28:32], scalar=1e-5, in1=stats[:, 16:20],
            op0=ALU.add, op1=ALU.subtract), inc='vp')
        W('s', vt)
        E('s', lambda s: s.activation(out=stats[:, 20:24], in_=stats[:, 20:24], func=AF.Ln))
        st_ = E('s', lambda s: s.activation(out=stats[:, 20:24], in_=stats[:, 20:24],
                                            func=AF.Exp, scale=nhalf[:]), inc='sp')
        W('v', st_)
        for ti in range(4):
            E('v', lambda v, ti=ti: v.tensor_scalar(
                out=xn16[:, ti * DM:(ti + 1) * DM], in0=xn16[:, ti * DM:(ti + 1) * DM],
                scalar1=stats[:, 24 + ti:25 + ti], scalar2=stats[:, 20 + ti:21 + ti],
                op0=ALU.subtract, op1=ALU.mult), inc='vp' if ti == 3 else None)
        W('t', pg.cur('vp'))
        for ti in range(4):
            transpose_block(xn16[:, ti * DM:(ti + 1) * DM], MB,
                            (lambda ti: lambda v, mb, ps: v.tensor_copy(
                                out=xnt[:, mb * T + ti * P:mb * T + (ti + 1) * P], in_=ps))(ti),
                            ti * MB)
        W('t', pg.cur('vp'))
        W('q', pg.cur('tp'))
        W('q', pg.cur('vp'))
        W('s', pg.cur('gp'))
        for mpair in range(16):
            sl = mpair % 2
            so = sl * 2 * MB * P
            W('q', pg.cur('tm'))
            wt = D(winr[:, so:so + 2 * MB * P], w_in16s[mpair], 'qw')
            W('t', wt)
            for m in range(2):
                mb = 2 * mpair + m
                pmi = mb % 2
                tt, pm = mm_chain(pmi,
                                  lambda k, so=so, m=m: winr[:, so + (k * 2 + m) * P:so + (k * 2 + m + 1) * P],
                                  lambda k: xnt[:, k * T:(k + 1) * T], MB)
                if mb < DB:
                    W('s', tt)
                    G['pm'][pmi] = E('s', lambda s, pm=pm, mb=mb: s.activation(
                        out=stage[:, mb * (T + 3) + 3:mb * (T + 3) + 3 + T], in_=pm[:],
                        func=AF.Identity, bias=binc[:, mb:mb + 1]), inc='sp')
                else:
                    W('s', tt)
                    G['pm'][pmi] = E('s', lambda s, pm=pm, mb=mb: s.activation(
                        out=srA[:, (mb - DB) * T:(mb - DB + 1) * T], in_=pm[:],
                        func=AF.Silu, bias=binc[:, mb:mb + 1]), inc='sp')
        W('t', pg.cur('vp'))
        W('t', pg.cur('sp'))

        for db in range(DB):
            pmi = db % 2
            so = db * (T + 3)
            half8 = (db % 2) * 4
            W('v', pg.cur('tm'))
            dt_ = None
            for tap in range(4):
                dt_ = E('v', lambda v, tap=tap, db=db, half8=half8: v.tensor_scalar(
                    out=diag[:, (half8 + tap) * P:(half8 + tap + 1) * P], in0=id16[:],
                    scalar1=wcv[:, db * 4 + tap:db * 4 + tap + 1], scalar2=None, op0=ALU.mult),
                    inc='vp' if tap == 3 else None)
            tt, pm = mm_chain(pmi,
                              lambda k, half8=half8: diag[:, (half8 + k) * P:(half8 + k + 1) * P],
                              lambda k, so=so: stage[:, so + k:so + k + T], 4, extra_wait=dt_)
            W('v', tt)
            G['pm'][pmi] = E('v', lambda v, pm=pm, db=db: v.tensor_scalar(
                out=spkA[:, db * T:(db + 1) * T], in0=pm[:],
                scalar1=thr[:, db:db + 1], scalar2=None, op0=ALU.is_ge), inc='vp')

        tl_t = E('v', lambda v: v.tensor_copy(
            out=tails[:].rearrange("p (db u) -> p db u", u=3),
            in_=stage[:].rearrange("p (db u) -> p db u", u=T + 3)[:, :, T:T + 3]), inc='vp')

        W('t', pg.cur('vp'))
        W('t', G['pm'][0])
        pm = pm0
        for k in range(DB):
            E('t', lambda t, k=k, pm=pm: t.matmul(
                out=pm[0:32, 0:T], lhsT=wxbcr[:, k * 32:(k + 1) * 32],
                rhs=spkA[:, k * T:(k + 1) * T], start=(k == 0), stop=False))
        tt = E('t', lambda t, pm=pm: t.matmul(
            out=pm[0:32, 0:T], lhsT=bxbr[:], rhs=ones_t[:], start=False, stop=True), inc='tm')
        W('v', tt)
        G['pm'][0] = E('v', lambda v, pm=pm: v.tensor_copy(
            out=bcb[:], in_=pm[0:32, :]), inc='vp')

        for jb in range(DB):
            sl = jb % 2
            so = sl * DB * P
            W('q', pg.cur('tm'))
            wt = D(wcmr[:, so:so + DB * P], w_cmb16s[jb], 'qw')
            pmi = jb % 2
            tt, pm = mm_chain(pmi,
                              lambda k, so=so: wcmr[:, so + k * P:so + (k + 1) * P],
                              lambda k: spkA[:, k * T:(k + 1) * T], DB, extra_wait=wt)
            W('s', tt)
            E('s', lambda s, pm=pm, jb=jb: s.activation(
                out=tmpa[:, (jb % 2) * T:(jb % 2 + 1) * T], in_=pm[:],
                func=AF.Exp, bias=bcmb[:, jb:jb + 1]))
            G['pm'][pmi] = E('s', lambda s, jb=jb: s.activation(
                out=d2A[:, jb * T:(jb + 1) * T], in_=tmpa[:, (jb % 2) * T:(jb % 2 + 1) * T],
                func=AF.Ln, bias=1.0), inc='sp')
        W('g', pg.cur('sp'))

        for db in range(DB):
            E('g', lambda g_, db=db: g_.tensor_tensor(
                out=wbA[:, db * T:(db + 1) * T], in0=d2A[:, db * T:(db + 1) * T],
                in1=spkA[:, db * T:(db + 1) * T], op=ALU.mult), inc='gp' if db == DB - 1 else None)

        wcmb_t = pg.cur('tm')

        W('v', pg.cur('gp'))
        W('t', pg.cur('vp'))
        grp_vp = {}
        hch = {}
        bx_done = {}
        for s_i in range(DS):
            soh = (s_i % 2) * 2 * MB * P
            W('q', hch[s_i - 2] if s_i >= 2 else None)
            wt_h = D(w1r[:, soh:soh + 2 * MB * P], w1s[s_i], 'qw')
            if s_i >= 2:
                W('v', bx_done[s_i - 2])
            for which, dst in [(0, brep), (1, crep)]:
                pri = which
                pr = (pr0, pr1)[pri]
                row = s_i + which * DS
                W('t', G['pr'][pri])
                tt = E('t', lambda t, pr=pr, row=row: t.matmul(
                    out=pr[:], lhsT=ind32[:, row * P:(row + 1) * P], rhs=bcb[:],
                    start=True, stop=True), inc='tm')
                W('v', tt)
                G['pr'][pri] = E('v', lambda v, pr=pr, dst=dst, s_i=s_i: v.tensor_copy(
                    out=dst[:, (s_i % 2) * T:(s_i % 2 + 1) * T], in_=pr[:]), inc='vp')
            rep_t = G['pr'][0]
            rsl = (s_i % 2) * T
            h_info = []
            for m in range(2):
                hb = 2 * s_i + m
                pbi = hb % 2
                tt_h, pbt = mm_chain(pbi,
                                     lambda k, soh=soh, m=m: w1r[:, soh + (k * 2 + m) * P:soh + (k * 2 + m + 1) * P],
                                     lambda k: xnt[:, k * T:(k + 1) * T], MB,
                                     extra_wait=wt_h if m == 0 else None, pair='pb')
                hch[s_i] = tt_h
                h_info.append((hb, pbi, tt_h, pbt))
            for db in range(DB):
                g = s_i * DB + db
                so = (g % 2) * T
                soa = (g % 4) * T
                if g >= 4:
                    W('s', grp_vp[g - 4])
                at = E('s', lambda s, soa=soa, db=db, s_i=s_i: s.activation(
                    out=abar[:, soa:soa + T], in_=d2A[:, db * T:(db + 1) * T],
                    func=AF.Exp, scale=acol[:, db * DS + s_i:db * DS + s_i + 1]), inc='sp')
                if db == 0:
                    W('g', rep_t)
                if g >= 4:
                    W('g', grp_vp[g - 4])
                W('g', at)
                bt = E('g', lambda g_, soa=soa, db=db, rsl=rsl: g_.tensor_tensor(
                    out=bxs[:, soa:soa + T], in0=wbA[:, db * T:(db + 1) * T],
                    in1=brep[:, rsl:rsl + T], op=ALU.mult), inc='gp')
                W('v', bt)
                E('v', lambda v, so=so, soa=soa, g=g: v.tensor_tensor_scan(
                    out=hsc[:, so:so + T], data0=abar[:, soa:soa + T], data1=bxs[:, soa:soa + T],
                    initial=hstate[:, g:g + 1], op0=ALU.mult, op1=ALU.add))
                if s_i == 0:
                    E('v', lambda v, so=so, db=db, rsl=rsl: v.tensor_tensor(
                        out=yac[:, db * T:(db + 1) * T], in0=hsc[:, so:so + T],
                        in1=crep[:, rsl:rsl + T], op=ALU.mult))
                else:
                    E('v', lambda v, so=so, rsl=rsl: v.tensor_tensor(
                        out=hcs[:, so:so + T], in0=hsc[:, so:so + T],
                        in1=crep[:, rsl:rsl + T], op=ALU.mult))
                    E('v', lambda v, so=so, db=db: v.tensor_tensor(
                        out=yac[:, db * T:(db + 1) * T], in0=yac[:, db * T:(db + 1) * T],
                        in1=hcs[:, so:so + T], op=ALU.add))
                grp_vp[g] = E('v', lambda v, so=so, g=g: v.tensor_copy(
                    out=hstate[:, g:g + 1], in_=hsc[:, so + T - 1:so + T]), inc='vp')
            bx_done[s_i] = pg.cur('gp')
            for hb, pbi, tt_h, pbt in h_info:
                W('s', tt_h)
                if hb == 0:
                    W('s', tl_t)
                G['pb'][pbi] = E('s', lambda s, pbt=pbt, hb=hb: s.activation(
                    out=hqb[:, hb * T:(hb + 1) * T], in_=pbt[:], func=AF.Relu,
                    bias=b1c[:, hb:hb + 1]), inc='sp')

        W('t', pg.cur('sp'))
        w2ch = {}
        for mb in range(MB):
            sl = mb % 2
            so = sl * HB * P
            W('q', w2ch[mb - 2] if mb >= 2 else wcmb_t)
            wt = D(w2r[:, so:so + HB * P], w2s[mb], 'qw')
            pbi = mb % 2
            tt, pb = mm_chain(pbi,
                              lambda k, so=so: w2r[:, so + k * P:so + (k + 1) * P],
                              lambda k: hqb[:, k * T:(k + 1) * T], HB,
                              extra_wait=wt, pair='pb')
            w2ch[mb] = tt
            W('v', tt)
            vt = E('v', lambda v, pb=pb, mb=mb: v.tensor_scalar(
                out=oqb[:, mb * T:(mb + 1) * T], in0=pb[:], scalar1=b2qc[:, mb:mb + 1],
                scalar2=None, op0=ALU.add), inc='vp')
            G['pb'][pbi] = vt
            W('s', vt)
            E('s', lambda s, mb=mb: s.activation(
                out=oqb[:, mb * T:(mb + 1) * T], in_=oqb[:, mb * T:(mb + 1) * T],
                func=AF.Silu), inc='sp')

        for db in range(DB):
            E('v', lambda v, db=db: v.scalar_tensor_tensor(
                out=yac[:, db * T:(db + 1) * T], in0=spkA[:, db * T:(db + 1) * T],
                scalar=dcol[:, db:db + 1], in1=yac[:, db * T:(db + 1) * T],
                op0=ALU.mult, op1=ALU.add), inc='vp' if db == DB - 1 else None)
        W('g', pg.cur('vp'))
        for db in range(DB):
            E('g', lambda g_, db=db: g_.tensor_tensor(
                out=wbA[:, db * T:(db + 1) * T], in0=yac[:, db * T:(db + 1) * T],
                in1=srA[:, db * T:(db + 1) * T], op=ALU.mult), inc='gp' if db == DB - 1 else None)
        W('t', pg.cur('gp'))

        for mb in range(MB):
            sl = mb % 2
            so = sl * DB * P
            W('q', pg.cur('tm'))
            wt = D(wor[:, so:so + DB * P], w_outs[mb], 'qw')
            pmi = mb % 2
            tt, pm = mm_chain(pmi,
                              lambda k, so=so: wor[:, so + k * P:so + (k + 1) * P],
                              lambda k: wbA[:, k * T:(k + 1) * T], DB, extra_wait=wt)
            W('v', tt)
            G['pm'][pmi] = E('v', lambda v, pm=pm, mb=mb: v.scalar_tensor_tensor(
                out=ob[:, mb * T:(mb + 1) * T], in0=pm[:], scalar=boutc[:, mb:mb + 1],
                in1=xnt[:, mb * T:(mb + 1) * T], op0=ALU.add, op1=ALU.add), inc='vp')

        W('s', pg.cur('vp'))
        W('s', ln2_t)
        W('t', G['pr'][0])
        W('t', G['pr'][1])
        sqdone = {}
        for mb in range(MB):
            if mb >= 2:
                W('s', sqdone[mb - 2])
            st_ = E('s', lambda s, mb=mb: s.activation(
                out=o2t[:, (mb % 2) * T:(mb % 2 + 1) * T], in_=ob[:, mb * T:(mb + 1) * T],
                func=AF.Square), inc='sp')
            W('t', st_)
            E('t', lambda t, mb=mb: t.matmul(
                out=pr0[0:1, :], lhsT=onek16[:], rhs=ob[:, mb * T:(mb + 1) * T],
                start=(mb == 0), stop=(mb == MB - 1)))
            sqdone[mb] = E('t', lambda t, mb=mb: t.matmul(
                out=pr1[0:1, :], lhsT=onek16[:], rhs=o2t[:, (mb % 2) * T:(mb % 2 + 1) * T],
                start=(mb == 0), stop=(mb == MB - 1)), inc='tm')
        ln2_t = pg.cur('tm')
        W('v', pg.cur('tm'))
        E('v', lambda v: v.tensor_scalar(out=st1[0:1, 0:T], in0=pr0[0:1, :], scalar1=1.0 / DM,
                                         scalar2=None, op0=ALU.mult))
        E('v', lambda v: v.tensor_scalar(out=st1[0:1, T:2 * T], in0=pr1[0:1, :], scalar1=1.0 / DM,
                                         scalar2=None, op0=ALU.mult), inc='vp')
        G['pr'][0] = G['pr'][1] = pg.cur('vp')
        E('v', lambda v: v.tensor_tensor(out=st2[0:1, 0:T], in0=st1[0:1, 0:T], in1=st1[0:1, 0:T],
                                         op=ALU.mult))
        E('v', lambda v: v.tensor_tensor(out=st1[0:1, T:2 * T], in0=st1[0:1, T:2 * T],
                                         in1=st2[0:1, 0:T], op=ALU.subtract))
        vt = E('v', lambda v: v.tensor_scalar(out=st1[0:1, T:2 * T], in0=st1[0:1, T:2 * T],
                                              scalar1=1e-5, scalar2=None, op0=ALU.add), inc='vp')
        W('s', vt)
        E('s', lambda s: s.activation(out=st1[0:1, T:2 * T], in_=st1[0:1, T:2 * T], func=AF.Ln))
        st_ = E('s', lambda s: s.activation(out=st1[0:1, T:2 * T], in_=st1[0:1, T:2 * T],
                                            func=AF.Exp, scale=nhalf[0:1, :]), inc='sp')
        W('v', st_)
        E('v', lambda v: v.tensor_copy(out=st2[0:1, 0:T], in_=st1[0:1, 0:T]))
        vt = E('v', lambda v: v.tensor_copy(out=st2[0:1, T:2 * T], in_=st1[0:1, T:2 * T]), inc='vp')
        W('t', vt)
        W('t', G['pb'][0])
        W('t', G['pb'][1])
        E('t', lambda t: t.matmul(out=pb0[:], lhsT=ones_r[:], rhs=st2[0:1, 0:T],
                                  start=True, stop=True))
        tt = E('t', lambda t: t.matmul(out=pb1[:], lhsT=ones_r[:], rhs=st2[0:1, T:2 * T],
                                       start=True, stop=True), inc='tm')
        W('v', tt)
        for mb in range(MB):
            E('v', lambda v, mb=mb: v.tensor_tensor(
                out=tmpa[:, 0:T], in0=ob[:, mb * T:(mb + 1) * T], in1=pb0[:], op=ALU.subtract))
            E('v', lambda v: v.tensor_tensor(
                out=tmpa[:, 0:T], in0=tmpa[:, 0:T], in1=pb1[:], op=ALU.mult))
            E('v', lambda v, mb=mb: v.tensor_scalar(
                out=tmpa[:, T:2 * T], in0=tmpa[:, 0:T], scalar1=gamc[:, mb:mb + 1],
                scalar2=betc[:, mb:mb + 1], op0=ALU.mult, op1=ALU.add))
            E('v', lambda v, mb=mb: v.tensor_scalar(
                out=sb16[:, mb * T:(mb + 1) * T], in0=tmpa[:, T:2 * T], scalar1=1.0,
                scalar2=None, op0=ALU.is_ge), inc='vp' if mb == MB - 1 else None)
        G['pb'][0] = G['pb'][1] = pg.cur('vp')

        W('v', pg.cur('sp'))
        for mb in range(MB):
            E('v', lambda v, mb=mb: v.tensor_tensor(
                out=oqb[:, mb * T:(mb + 1) * T], in0=oqb[:, mb * T:(mb + 1) * T],
                in1=sb16[:, mb * T:(mb + 1) * T], op=ALU.mult), inc='vp' if mb == MB - 1 else None)

        W('v', pg.cur('gp'))
        W('t', pg.cur('vp'))
        for mb in range(MB):
            transpose_block(oqb[:, mb * T:(mb + 1) * T], 4,
                            (lambda mb: lambda v, ti, ps: v.tensor_copy(
                                out=sgt[:, ti * DM + mb * P:ti * DM + (mb + 1) * P], in_=ps))(mb),
                            mb * 4)
        for half in range(2):
            W('q', pg.cur('qo'))
            for i in range(2):
                ti = half * 2 + i
                xt = D(xbuf[:, i * DM:(i + 1) * DM], xin[t0 + ti * P:t0 + (ti + 1) * P, :], 'qx')
            W('v', xt)
            ot = None
            for i in range(2):
                ti = half * 2 + i
                ot = E('v', lambda v, i=i, ti=ti: v.tensor_tensor(
                    out=xbuf[:, i * DM:(i + 1) * DM], in0=xbuf[:, i * DM:(i + 1) * DM],
                    in1=sgt[:, ti * DM:(ti + 1) * DM], op=ALU.add), inc='vp')
            W('q', ot)
            for i in range(2):
                ti = half * 2 + i
                D(yout[t0 + ti * P:t0 + (ti + 1) * P, :], xbuf[:, i * DM:(i + 1) * DM], 'qo')
            W('v', pg.cur('qo'))

    W('q', pg.cur('qo'))
    W('v', pg.cur('qo'))

    with nc.Block() as block:
        @block.sync
        def _(q):
            for eng, fn in pg.cmds:
                if eng == 'q':
                    fn(q)

        @block.vector
        def _(v):
            for eng, fn in pg.cmds:
                if eng == 'v':
                    fn(v)

        @block.scalar
        def _(s):
            for eng, fn in pg.cmds:
                if eng == 's':
                    fn(s)

        @block.tensor
        def _(t):
            for eng, fn in pg.cmds:
                if eng == 't':
                    fn(t)

        @block.gpsimd
        def _(g):
            for eng, fn in pg.cmds:
                if eng == 'g':
                    fn(g)

    return nc


_AUX = None


def _aux():
    global _AUX
    if _AUX is None:
        a = np.zeros((P, P + 1), np.float32)
        a[:, 0:P] = np.eye(P, dtype=np.float32)
        a[:, P] = 1.0
        _AUX = a
    return _AUX


def kernel(L0_attn_x, L0_attn_W_in, L0_attn_b_in, L0_attn_W_conv, L0_attn_b_conv,
           L0_attn_W_x, L0_attn_b_x, L0_attn_W_dt, L0_attn_b_dt, L0_attn_A_log,
           L0_attn_D_param, L0_attn_W_out, L0_attn_b_out, L0_attn_gamma, L0_attn_beta,
           L0_ffn_w1, L0_ffn_b1, L0_ffn_w2, L0_ffn_b2):
    nc = build_nc()
    base = {
        "w_in": np.ascontiguousarray(L0_attn_W_in, np.float32),
        "b_in": np.ascontiguousarray(L0_attn_b_in, np.float32),
        "w_conv": np.ascontiguousarray(np.asarray(L0_attn_W_conv).reshape(DI, 4), np.float32),
        "b_conv": np.ascontiguousarray(L0_attn_b_conv, np.float32),
        "w_x": np.ascontiguousarray(L0_attn_W_x, np.float32),
        "b_x": np.ascontiguousarray(L0_attn_b_x, np.float32),
        "w_dt": np.ascontiguousarray(L0_attn_W_dt, np.float32),
        "b_dt": np.ascontiguousarray(L0_attn_b_dt, np.float32),
        "a_log": np.ascontiguousarray(L0_attn_A_log, np.float32),
        "d_par": np.ascontiguousarray(L0_attn_D_param, np.float32),
        "w_out": np.ascontiguousarray(L0_attn_W_out, np.float32),
        "b_out": np.ascontiguousarray(L0_attn_b_out, np.float32),
        "gam": np.ascontiguousarray(L0_attn_gamma, np.float32),
        "bet": np.ascontiguousarray(L0_attn_beta, np.float32),
        "w1": np.ascontiguousarray(L0_ffn_w1, np.float32),
        "b1": np.ascontiguousarray(L0_ffn_b1, np.float32),
        "w2": np.ascontiguousarray(L0_ffn_w2, np.float32),
        "b2": np.ascontiguousarray(L0_ffn_b2, np.float32),
        "aux": _aux(),
    }
    x = np.ascontiguousarray(L0_attn_x, np.float32)
    in_maps = [dict(base, x=np.ascontiguousarray(x[b])) for b in range(8)]
    res = run_bass_kernel_spmd(nc, in_maps, core_ids=list(range(8)))
    return np.stack([np.asarray(r["out"], np.float32) for r in res.results], axis=0)
